# version 03 (final)
# baseline (speedup 1.0000x reference)
import numpy as np
import ml_dtypes
import concourse.bass as bass
import concourse.mybir as mybir
from concourse.bass_utils import run_bass_kernel_spmd
from contextlib import ExitStack

F32 = mybir.dt.float32
BF16 = mybir.dt.bfloat16
AF = mybir.ActivationFunctionType
ALU = mybir.AluOpType
AX = mybir.AxisListType

S = 2048
B_LOC = 2
D = 1024
FF = 4096
E = 8
NST = 16
KCV = 4
NCORES = 8
EPS = 1e-5

DJ = D // 128
FJ = FF // 128
FC = FF // 512


class Sems:
    def __init__(self, nc, stack):
        self.nc, self.stack = nc, stack
        self.h, self.c = {}, {}

    def new(self, *names):
        for name in names:
            self.h[name] = self.stack.enter_context(self.nc.semaphore(name))
            self.c[name] = 0

    def inc(self, instr, name, n=1):
        instr.then_inc(self.h[name], n)
        self.c[name] += n
        return self.c[name]

    def wait(self, eng, name, v=None):
        v = self.c[name] if v is None else v
        if v > 0:
            eng.wait_ge(self.h[name], v)


def build_program():
    T = B_LOC * S
    NT = T // 128
    TBLK = min(512, T)
    NTB = T // TBLK
    JB = TBLK // 128
    T2 = TBLK // 512
    SCH = max(1, S // 512)
    SW = min(512, S)
    TCH = T // SW

    nc = bass.Bass("TRN2", target_bir_lowering=False, debug=False)
    dram = lambda name, shape, dt, kind="Internal": nc.dram_tensor(
        name, shape, dt, kind=kind).ap()

    x_h = dram("x", [T, D], F32, "ExternalInput")
    w_inT = dram("w_inT", [D, 2 * D], F32, "ExternalInput")
    w_outT = dram("w_outT", [D, D], F32, "ExternalInput")
    w_xT_r = dram("w_xT_r", [128, DJ * 2 * NST], F32, "ExternalInput")
    w_dtT = dram("w_dtT", [NST, D], F32, "ExternalInput")
    b_dt_r = dram("b_dt_r", [128, DJ], F32, "ExternalInput")
    a_log_r = dram("a_log_r", [128, DJ * NST], F32, "ExternalInput")
    conv_r = dram("conv_r", [128, DJ * KCV], F32, "ExternalInput")
    rw_r = dram("rw_r", [128, DJ * E], F32, "ExternalInput")
    rb_rep = dram("rb_rep", [128, E], F32, "ExternalInput")
    lng_r = dram("lng_r", [128, DJ * E], F32, "ExternalInput")
    lnb_r = dram("lnb_r", [128, DJ * E], F32, "ExternalInput")
    b1_r = dram("b1_r", [128, FJ * E], F32, "ExternalInput")
    b2_h = dram("b2", [E, D], F32, "ExternalInput")
    w1_h = dram("w1", [E * D, FF], F32, "ExternalInput")
    w2_h = dram("w2", [E * FF, D], F32, "ExternalInput")
    ident_h = dram("ident", [128, 128], F32, "ExternalInput")
    identbf_h = dram("identbf", [128, 128], BF16, "ExternalInput")
    oh16_h = dram("oh16", [NST, NST * 128], BF16, "ExternalInput")
    oh8_h = dram("oh8", [E, E * 128], BF16, "ExternalInput")
    fvals_h = dram("fvals", [128, 8], F32, "ExternalInput")

    out_h = dram("out", [T, D], F32, "ExternalOutput")

    w1s = dram("w1s", [E * FC * 128, 8 * 512], BF16)
    w2s = dram("w2s", [E * DJ * 128, FJ * 128], BF16)
    winTs = dram("winTs", [16 * 128, DJ * 128], BF16)
    woutTs = dram("woutTs", [DJ * 128, D], BF16)
    b2s = dram("b2s", [E, D], BF16)
    wxTs = dram("wxTs", [128, DJ * 2 * NST], BF16)
    wdtTs = dram("wdtTs", [NST, D], BF16)
    xn_sp = dram("xn_sp", [T, D], BF16)
    xc_sp = dram("xc_sp", [D, T], F32)
    dt_sp = dram("dt_sp", [D, T], F32)
    sz_sp = dram("sz_sp", [D, T], BF16)
    at_sp = dram("at_sp", [D, T], BF16)

    stack = ExitStack()
    with stack:
        sb = lambda name, shape, dt: stack.enter_context(nc.sbuf_tensor(name, shape, dt))
        psa = lambda name, shape, dt: stack.enter_context(nc.psum_tensor(name, shape, dt))
        SM = Sems(nc, stack)

        ident = sb("identsb", [128, 128], F32)
        identbf = sb("identbfsb", [128, 128], BF16)
        oh8 = sb("oh8sb", [E, E * 128], BF16)
        rw_sb = sb("rw_sb", [128, DJ * E], F32)
        rb_sb = sb("rb_sb", [128, E], F32)
        conv_sb = sb("conv_sb", [128, DJ * KCV], F32)
        bdt_sb = sb("bdt_sb", [128, DJ], F32)
        a_sb = sb("a_sb", [128, DJ * NST], F32)
        lng_sb = sb("lng_sb", [128, DJ * E], F32)
        lnb_sb = sb("lnb_sb", [128, DJ * E], F32)
        b1_sb = sb("b1_sb", [128, FJ * E], F32)
        b2_sb = sb("b2_sb", [E, D], BF16)
        wxT_sb = sb("wxT_sb", [128, DJ * 2 * NST], BF16)
        wdtT_sb = sb("wdtT_sb", [NST, D], BF16)
        mu2_all = sb("mu2_all", [128, NT], F32)
        rs2_all = sb("rs2_all", [128, NT], F32)
        gatesTbf = sb("gatesTbf", [E, T], BF16)
        fvals_sb = sb("fvals_sb", [128, 8], F32)
        for _fi, _fv in enumerate([EPS, -0.5, -1.0, 1.0]):
            nc.const_aps.register(F32, _fv, fvals_sb[:, _fi:_fi + 1])

        PS_A = [psa(f"PS_A{i}", [128, 512], F32) for i in range(3)]
        PS_B = [psa(f"PS_B{i}", [128, 1024], F32) for i in range(2)]
        PS_T = psa("PS_T", [128, max(JB, DJ) * 128], BF16)
        oh16 = stack.enter_context(nc.sbuf_tensor("oh16sb", [NST, NST * 128], BF16))
        bm_bf = stack.enter_context(nc.sbuf_tensor("bm_bf", [NST, T], BF16))

        blk = stack.enter_context(nc.Block())

        SM.new("c_ld0", "c_ld1", "c_st0", "c_st1", "k_ld", "k2_ld",
               "vch", "sch",
               "x_ld0", "x_ld1", "x_fr0", "x_fr1",
               "p0_var", "p0_rs", "p0_xn", "p0_sp", "p0_tp", "p0_ev",
               "p0_lg", "p0_lgev", "p0_dif", "p0_w2g", "p0_g", "p0_gt", "p0_gev",
               "wi_ld0", "wi_ld1",
               "pa_mm", "pa_ev", "pa_evz", "pa_cv", "pa_sil", "pa_xbf", "pa_sp",
               "pa_dl", "pa_db", "pa_dlev", "pa_dbf", "pa_dmm",
               "pa_spu", "pa_spl", "pa_spr", "pa_dte",
               "sc_ld0", "sc_ld1", "sc_fr0", "sc_fr1",
               "sc_a", "sc_brep", "sc_bx", "sc_scan",
               "wo_ld", "wo_mm", "wo_ev", "wo_sp",
               "f_xn", "f_w10", "f_w11", "f_w20", "f_w21",
               "f_xh", "f_tp", "f_xhev", "f_init", "f_initev",
               "f_gr", "f_grev", "f_h", "f_mm1", "f_gel", "f_mm2", "f_fma",
               "f_at", "f_aadd", "f_otp", "f_xr0", "f_xr1", "f_oev", "f_out0", "f_out1",
               "s_ld", "s_ach", "s_scan", "s_cc", "s_gy", "s_br", "s_brev",
               "f_wo", "f_woev")

        conv_jobs = []
        for fb in range(16):
            conv_jobs.append((w_inT[:, fb * 128:(fb + 1) * 128]
                              .rearrange("(j p) f -> p j f", p=128),
                              winTs[fb * 128:(fb + 1) * 128, :], 128, DJ * 128))
        conv_jobs.append((w_xT_r, wxTs, 128, DJ * 2 * NST))
        conv_jobs.append((w_dtT, wdtTs, NST, D))
        conv_jobs.append((b2_h, b2s, E, D))
        for dj in range(DJ):
            conv_jobs.append((w_outT[dj * 128:(dj + 1) * 128, :],
                              woutTs[dj * 128:(dj + 1) * 128, :], 128, D))
        for e in range(E):
            for fc in range(FC):
                conv_jobs.append((w1_h[e * D:(e + 1) * D, fc * 512:(fc + 1) * 512]
                                  .rearrange("(j p) f -> p j f", p=128),
                                  w1s[(e * FC + fc) * 128:(e * FC + fc + 1) * 128, :],
                                  128, 8 * 512))
        for e in range(E):
            for dj in range(DJ):
                conv_jobs.append((w2_h[e * FF:(e + 1) * FF, dj * 128:(dj + 1) * 128]
                                  .rearrange("(j p) d -> p j d", p=128),
                                  w2s[(e * DJ + dj) * 128:(e * DJ + dj + 1) * 128, :],
                                  128, FJ * 128))
        n_conv = len(conv_jobs)

        cvt_scope = ExitStack()
        cvt_buf = [cvt_scope.enter_context(
            nc.sbuf_tensor(f"cvt{i}", [128, 4096], BF16)) for i in range(2)]

        conv_done = [None] * n_conv

        @blk.gpsimd
        def _(g):
            for i, (src, dst, p, w) in enumerate(conv_jobs):
                sl = i % 2
                if i >= 2:
                    SM.wait(g, f"c_st{sl}", 16 * (i // 2))
                SM.inc(g.dma_start(out=cvt_buf[sl][0:p, 0:w], in_=src), f"c_ld{sl}", 16)
                SM.wait(g, f"c_ld{sl}", 16 * (i // 2 + 1))
                v = SM.inc(g.dma_start(out=dst, in_=cvt_buf[sl][0:p, 0:w]), f"c_st{sl}", 16)
                conv_done[i] = (f"c_st{sl}", v)

        def wait_conv(eng, upto):
            best = {}
            for i in range(upto):
                s, v = conv_done[i]
                best[s] = max(best.get(s, 0), v)
            for s, v in best.items():
                eng.wait_ge(SM.h[s], v)

        @blk.sync
        def _(sy):
            for dst, src in [(ident, ident_h), (identbf, identbf_h),
                             (oh16, oh16_h), (oh8, oh8_h),
                             (rw_sb, rw_r), (rb_sb, rb_rep),
                             (conv_sb, conv_r), (bdt_sb, b_dt_r),
                             (a_sb, a_log_r), (lng_sb, lng_r),
                             (lnb_sb, lnb_r), (b1_sb, b1_r),
                             (fvals_sb, fvals_h)]:
                SM.inc(sy.dma_start(out=dst[:], in_=src), "k_ld", 16)
            wait_conv(sy, 19)
            SM.inc(sy.dma_start(out=wxT_sb[:], in_=wxTs), "k2_ld", 16)
            SM.inc(sy.dma_start(out=wdtT_sb[:], in_=wdtTs), "k2_ld", 16)
            SM.inc(sy.dma_start(out=b2_sb[:], in_=b2s), "k2_ld", 16)

        K_LD_T = SM.c["k_ld"]

        @blk.scalar
        def _(sc):
            SM.wait(sc, "k_ld", K_LD_T)
            SM.inc(sc.activation(out=a_sb[:], in_=a_sb[:], func=AF.Exp), "sch")

        @blk.vector
        def _(v):
            SM.wait(v, "sch", 1)
            SM.inc(v.tensor_scalar_mul(a_sb[:], a_sb[:], -1.0), "vch")

        A_READY = SM.c["vch"]

        ph0a = ExitStack()
        xn_dt = ph0a.enter_context(nc.sbuf_tensor("xn_dt", [128, DJ * T], BF16))
        ph0 = ExitStack()
        p0sb = lambda name, shape, dt: ph0.enter_context(nc.sbuf_tensor(name, shape, dt))
        x_tile = [p0sb(f"x_tile{i}", [128, D], F32) for i in range(2)]
        xn_f32 = p0sb("xn_f32", [128, D], F32)
        sq_t = p0sb("sq_t", [128, D], F32)
        xnT_f32 = p0sb("xnT_f32", [128, D], F32)
        l_sb = p0sb("l_sb", [128, E], F32)
        msk_sb = p0sb("msk_sb", [128, E], F32)
        xn_bf = p0sb("xn_bf", [128, D], BF16)
        vtmp = p0sb("vtmp", [128, 8], F32)
        var1_t = p0sb("var1_t", [128, 1], F32)
        var2_t = p0sb("var2_t", [128, 1], F32)
        rs1_t = p0sb("rs1_t", [128, 1], F32)
        lnv_t = p0sb("lnv_t", [128, 1], F32)
        m1_t = p0sb("m1_t", [128, 1], F32)
        m2_t = p0sb("m2_t", [128, 1], F32)
        dif_all = p0sb("dif_all", [128, NT], F32)
        w1g = p0sb("w1g", [128, NT], F32)
        w2g = p0sb("w2g", [128, NT], F32)
        eq1_all = p0sb("eq1_all", [128, NT * E], F32)
        eq2_all = p0sb("eq2_all", [128, NT * E], F32)
        gates_sb = p0sb("gates_sb", [128, NT * E], F32)

        @blk.sync
        def _(sy):
            for j in range(NT):
                sl = j % 2
                if j >= 2:
                    sy.wait_ge(SM.h[f"x_fr{sl}"], j // 2)
                SM.inc(sy.dma_start(out=x_tile[sl][:],
                                    in_=x_h[j * 128:(j + 1) * 128, :]), f"x_ld{sl}", 16)

        @blk.vector
        def _(v):
            SM.wait(v, "k_ld", K_LD_T)
            for j in range(NT):
                sl = j % 2
                v.wait_ge(SM.h[f"x_ld{sl}"], 16 * (j // 2 + 1))
                if j >= 1:
                    v.wait_ge(SM.h["p0_rs"], 2 * j - 1)
                SM.inc(v.reduce_sum(vtmp[:, 0:1], x_tile[sl][:], axis=AX.X), "vch")
                SM.inc(v.tensor_mul(sq_t[:], x_tile[sl][:], x_tile[sl][:]), "vch")
                SM.wait(v, "vch")
                SM.inc(v.reduce_sum(vtmp[:, 1:2], sq_t[:], axis=AX.X), "vch")
                SM.wait(v, "vch")
                SM.inc(v.tensor_scalar_mul(vtmp[:, 2:3], vtmp[:, 0:1], 1.0 / D), "vch")
                SM.inc(v.tensor_scalar_mul(vtmp[:, 3:4], vtmp[:, 1:2], 1.0 / D), "vch")
                SM.wait(v, "vch")
                SM.inc(v.tensor_mul(vtmp[:, 4:5], vtmp[:, 2:3], vtmp[:, 2:3]), "vch")
                SM.wait(v, "vch")
                SM.inc(v.tensor_sub(var1_t[:], vtmp[:, 3:4], vtmp[:, 4:5]),
                       "p0_var")
                v.wait_ge(SM.h["p0_rs"], 2 * j + 1)
                if j >= 1:
                    v.wait_ge(SM.h["p0_sp"], 16 * j)
                    v.wait_ge(SM.h["p0_tp"], j)
                SM.inc(v.tensor_scalar(xn_f32[:], x_tile[sl][:], vtmp[:, 2:3],
                                       rs1_t[:], ALU.subtract, ALU.mult),
                       f"x_fr{sl}")
                v.wait_ge(SM.h[f"x_fr{sl}"], j // 2 + 1)
                SM.inc(v.tensor_copy(xn_bf[:], xn_f32[:]), "p0_xn")
                SM.inc(v.reduce_sum(vtmp[:, 5:6], xn_f32[:], axis=AX.X), "vch")
                SM.inc(v.tensor_mul(sq_t[:], xn_f32[:], xn_f32[:]), "vch")
                SM.wait(v, "vch")
                SM.inc(v.reduce_sum(vtmp[:, 6:7], sq_t[:], axis=AX.X), "vch")
                SM.wait(v, "vch")
                SM.inc(v.tensor_scalar_mul(mu2_all[:, j:j + 1], vtmp[:, 5:6], 1.0 / D),
                       "vch")
                SM.inc(v.tensor_scalar_mul(vtmp[:, 6:7], vtmp[:, 6:7], 1.0 / D),
                       "vch")
                SM.wait(v, "vch")
                SM.inc(v.tensor_mul(vtmp[:, 7:8], mu2_all[:, j:j + 1],
                                    mu2_all[:, j:j + 1]), "vch")
                SM.wait(v, "vch")
                if j >= 1:
                    v.wait_ge(SM.h["p0_rs"], 2 * j)
                SM.inc(v.tensor_sub(var2_t[:], vtmp[:, 6:7], vtmp[:, 7:8]),
                       "p0_var")
                v.wait_ge(SM.h["p0_tp"], j + 1)
                if j >= 1:
                    v.wait_ge(SM.h["p0_lg"], j)
                SM.inc(v.tensor_copy(xnT_f32[:, 0:512], PS_B[j % 2][:, 0:512]), "vch")
                SM.inc(v.tensor_copy(xnT_f32[:, 512:D], PS_B[j % 2][:, 512:D]), "vch")
                SM.wait(v, "vch")
                for i in range(DJ):
                    tick = "p0_ev" if i == DJ - 1 else "vch"
                    SM.inc(v.tensor_copy(
                        xn_dt[:, T * i + 128 * j:T * i + 128 * (j + 1)],
                        xnT_f32[:, 128 * i:128 * (i + 1)]), tick)
                v.wait_ge(SM.h["p0_lg"], j + 1)
                SM.inc(v.tensor_add(l_sb[:], PS_A[j % 3][:, 0:E], rb_sb[:]), "p0_lgev")
                SM.wait(v, "vch")
                v.wait_ge(SM.h["p0_lgev"], j + 1)
                SM.inc(v.reduce_max(m1_t[:], l_sb[:], axis=AX.X), "vch")
                SM.wait(v, "vch")
                SM.inc(v.tensor_scalar(eq1_all[:, E * j:E * (j + 1)], l_sb[:],
                                       m1_t[:], None, ALU.is_ge), "vch")
                SM.wait(v, "vch")
                SM.inc(v.scalar_tensor_tensor(msk_sb[:], eq1_all[:, E * j:E * (j + 1)],
                                              -1e30, l_sb[:], ALU.mult, ALU.add), "vch")
                SM.wait(v, "vch")
                SM.inc(v.reduce_max(m2_t[:], msk_sb[:], axis=AX.X), "vch")
                SM.wait(v, "vch")
                SM.inc(v.tensor_scalar(eq2_all[:, E * j:E * (j + 1)], msk_sb[:],
                                       m2_t[:], None, ALU.is_ge), "vch")
                SM.inc(v.tensor_sub(dif_all[:, j:j + 1], m2_t[:], m1_t[:]),
                       "p0_dif" if j == NT - 1 else "vch")

        @blk.scalar
        def _(sc):
            for j in range(NT):
                sc.wait_ge(SM.h["p0_var"], 2 * j + 1)
                if j >= 1:
                    sc.wait_ge(SM.h["p0_xn"], j)
                    sc.wait_ge(SM.h["p0_rs"], 2 * j)
                SM.inc(sc.activation(out=lnv_t[:], in_=var1_t[:],
                                     func=AF.Ln, bias=EPS), "sch")
                SM.wait(sc, "sch")
                SM.inc(sc.activation(out=rs1_t[:], in_=lnv_t[:],
                                     func=AF.Exp, scale=-0.5), "p0_rs")
                sc.wait_ge(SM.h["p0_var"], 2 * j + 2)
                sc.wait_ge(SM.h["p0_rs"], 2 * j + 1)
                SM.inc(sc.activation(out=lnv_t[:], in_=var2_t[:],
                                     func=AF.Ln, bias=EPS), "sch")
                SM.wait(sc, "sch")
                SM.inc(sc.activation(out=rs2_all[:, j:j + 1], in_=lnv_t[:],
                                     func=AF.Exp, scale=-0.5), "p0_rs")
                sc.wait_ge(SM.h["p0_xn"], j + 1)
                SM.inc(sc.dma_start(out=xn_sp[j * 128:(j + 1) * 128, :], in_=xn_bf[:]),
                       "p0_sp", 16)
            sc.wait_ge(SM.h["p0_dif"], 1)
            SM.inc(sc.activation(out=w2g[:, 0:NT], in_=dif_all[:, 0:NT],
                                 func=AF.Sigmoid), "p0_w2g")

        @blk.tensor
        def _(t):
            SM.wait(t, "k_ld", K_LD_T)
            for j in range(NT):
                t.wait_ge(SM.h["p0_xn"], j + 1)
                if j >= 2:
                    t.wait_ge(SM.h["p0_ev"], j - 1)
                for i in range(DJ):
                    ins = t.transpose(PS_B[j % 2][:, 128 * i:128 * (i + 1)],
                                      xn_f32[:, 128 * i:128 * (i + 1)], ident[:])
                    if i == DJ - 1:
                        SM.inc(ins, "p0_tp")
                t.wait_ge(SM.h["p0_ev"], j + 1)
                if j >= 3:
                    t.wait_ge(SM.h["p0_lgev"], j - 2)
                for i in range(DJ):
                    ins = t.matmul(PS_A[j % 3][:, 0:E],
                                   xnT_f32[:, 128 * i:128 * (i + 1)],
                                   rw_sb[:, E * i:E * (i + 1)],
                                   start=(i == 0), stop=(i == DJ - 1))
                SM.inc(ins, "p0_lg")

        @blk.vector
        def _(v):
            v.wait_ge(SM.h["p0_w2g"], 1)
            SM.inc(v.tensor_scalar(w1g[:, 0:NT], w2g[:, 0:NT], -1.0, 1.0,
                                   ALU.mult, ALU.add), "vch")
            SM.wait(v, "vch")
            for j in range(NT):
                SM.inc(v.tensor_scalar(gates_sb[:, E * j:E * (j + 1)],
                                       eq1_all[:, E * j:E * (j + 1)],
                                       w1g[:, j:j + 1], None, ALU.mult), "vch")
                SM.wait(v, "vch")
                SM.inc(v.scalar_tensor_tensor(gates_sb[:, E * j:E * (j + 1)],
                                              eq2_all[:, E * j:E * (j + 1)],
                                              w2g[:, j:j + 1],
                                              gates_sb[:, E * j:E * (j + 1)],
                                              ALU.mult, ALU.add), "p0_g")
            for j in range(NT):
                v.wait_ge(SM.h["p0_gt"], j + 1)
                SM.inc(v.tensor_copy(gatesTbf[0:E, 128 * j:128 * (j + 1)],
                                     PS_A[j % 3][0:E, 0:128]), "p0_gev")

        @blk.tensor
        def _(t):
            for j in range(NT):
                t.wait_ge(SM.h["p0_g"], j + 1)
                if j >= 3:
                    t.wait_ge(SM.h["p0_gev"], j - 2)
                SM.inc(t.transpose(PS_A[j % 3][0:E, 0:128],
                                   gates_sb[:, E * j:E * (j + 1)], ident[:]),
                       "p0_gt")

        P0_SP_T = SM.c["p0_sp"]

        ph0.close()
        phA = ExitStack()
        pasb = lambda name, shape, dt: phA.enter_context(nc.sbuf_tensor(name, shape, dt))
        winT_buf = [pasb(f"winT{i}", [128, DJ * 128], BF16) for i in range(2)]
        pad_sb = pasb("pad_sb", [128, S + 3], F32)
        xc_acc = pasb("xc_acc", [128, S], F32)
        xc_sil = pasb("xc_sil", [128, S], F32)
        xc_bf = pasb("xc_bf", [128, S], BF16)
        dtl_sb = pasb("dtl_sb", [NST, T], F32)
        bmx_sb = pasb("bmx_sb", [NST, T], F32)
        dtl_bf = pasb("dtl_bf", [NST, T], BF16)
        sz_half = pasb("sz_half", [128, S], BF16)
        dt_half = pasb("dt_half", [128, S], F32)
        sp_u = pasb("sp_u", [128, 512], F32)
        sp_l = pasb("sp_l", [128, 512], F32)
        sp_r = pasb("sp_r", [128, 512], F32)

        @blk.sync
        def _(sy):
            sy.wait_ge(SM.h["p0_gev"], NT)
            sy.wait_ge(SM.h["p0_sp"], 16 * NT)
            sy.wait_ge(SM.h["x_fr0"], (NT + 1) // 2)
            sy.wait_ge(SM.h["x_fr1"], NT // 2)
            for fb in range(16):
                sl = fb % 2
                wait_conv(sy, fb + 1)
                if fb >= 2:
                    sy.wait_ge(SM.h["pa_mm"], B_LOC * SCH * (fb - 1))
                SM.inc(sy.dma_start(out=winT_buf[sl][:],
                                    in_=winTs[fb * 128:(fb + 1) * 128, :]),
                       f"wi_ld{sl}", 16)

        @blk.tensor
        def _(t):
            SM.wait(t, "k2_ld", 48)
            t.wait_ge(SM.h["p0_ev"], NT)
            idx = 0
            for fb in range(16):
                sl = fb % 2
                t.wait_ge(SM.h[f"wi_ld{sl}"], 16 * (fb // 2 + 1))
                for b in range(B_LOC):
                    for tc in range(SCH):
                        if idx >= 3:
                            prev = idx - 3
                            nxin = DJ * B_LOC * SCH
                            if prev < nxin:
                                t.wait_ge(SM.h["pa_ev"], prev + 1)
                            else:
                                t.wait_ge(SM.h["pa_evz"], prev - nxin + 1)
                        pst = PS_A[idx % 3]
                        for dj in range(DJ):
                            ins = t.matmul(
                                pst[:, 0:SW],
                                winT_buf[sl][:, 128 * dj:128 * (dj + 1)],
                                xn_dt[:, T * dj + S * b + SW * tc:
                                      T * dj + S * b + SW * (tc + 1)],
                                start=(dj == 0), stop=(dj == DJ - 1))
                        SM.inc(ins, "pa_mm")
                        idx += 1
                    if fb < DJ:
                        dj = fb
                        q = dj * B_LOC + b
                        t.wait_ge(SM.h["pa_xbf"], q + 1)
                        for tc in range(SCH):
                            c = q * SCH + tc
                            if c >= 1:
                                t.wait_ge(SM.h["pa_dlev"], 2 * c)
                            SM.inc(t.matmul(PS_B[0][0:NST, 0:SW],
                                            wxT_sb[:, 32 * dj:32 * dj + NST],
                                            xc_bf[:, SW * tc:SW * (tc + 1)],
                                            start=True, stop=True), "pa_dl")
                            SM.inc(t.matmul(PS_B[1][0:NST, 0:SW],
                                            wxT_sb[:, 32 * dj + NST:32 * dj + 2 * NST],
                                            xc_bf[:, SW * tc:SW * (tc + 1)],
                                            start=True, stop=True), "pa_db")
            t.wait_ge(SM.h["pa_dbf"], 1)
            for didx in range(DJ * TCH):
                dj = didx // TCH
                tc = didx % TCH
                if didx < 3:
                    t.wait_ge(SM.h["pa_evz"], DJ * B_LOC * SCH)
                else:
                    t.wait_ge(SM.h["pa_spu"], didx - 2)
                    t.wait_ge(SM.h["pa_spr"], didx - 2)
                SM.inc(t.matmul(PS_A[didx % 3][:, 0:SW],
                                wdtT_sb[:, 128 * dj:128 * (dj + 1)],
                                dtl_bf[0:NST, SW * tc:SW * (tc + 1)],
                                start=True, stop=True), "pa_dmm")

        @blk.vector
        def _(v):
            SM.wait(v, "k_ld", K_LD_T)
            v.wait_ge(SM.h["p0_gt"], NT)
            v.wait_ge(SM.h["p0_gev"], NT)
            v.wait_ge(SM.h["p0_sp"], 16 * NT)
            SM.inc(v.memset(pad_sb[:, 0:3], 0.0), "vch")
            SM.inc(v.memset(dtl_sb[:], 0.0), "vch")
            SM.inc(v.memset(bmx_sb[:], 0.0), "vch")
            SM.wait(v, "vch")
            for q in range(DJ * B_LOC):
                dj, b = q // B_LOC, q % B_LOC
                fb = dj
                for tc in range(SCH):
                    idx = (fb * B_LOC + b) * SCH + tc
                    v.wait_ge(SM.h["pa_mm"], idx + 1)
                    if q >= 1:
                        v.wait_ge(SM.h["pa_cv"], q)
                    SM.inc(v.tensor_copy(pad_sb[:, 3 + SW * tc:3 + SW * (tc + 1)],
                                         PS_A[idx % 3][:, 0:SW]), "pa_ev")
                if q >= 1:
                    v.wait_ge(SM.h["pa_sil"], q)
                SM.wait(v, "vch")
                v.wait_ge(SM.h["pa_ev"], SM.c["pa_ev"])
                SM.inc(v.tensor_scalar(xc_acc[:], pad_sb[:, 0:S],
                                       conv_sb[:, KCV * dj:KCV * dj + 1], None,
                                       ALU.mult), "vch")
                for k in range(1, KCV):
                    SM.wait(v, "vch")
                    SM.inc(v.scalar_tensor_tensor(
                        xc_acc[:], pad_sb[:, k:k + S],
                        conv_sb[:, KCV * dj + k:KCV * dj + k + 1],
                        xc_acc[:], ALU.mult, ALU.add),
                        "pa_cv" if k == KCV - 1 else "vch")
                v.wait_ge(SM.h["pa_sil"], q + 1)
                if q >= 1:
                    v.wait_ge(SM.h["pa_dl"], q * SCH)
                SM.inc(v.tensor_copy(xc_bf[:], xc_sil[:]), "pa_xbf")
                for tc in range(SCH):
                    c = q * SCH + tc
                    v.wait_ge(SM.h["pa_dl"], c + 1)
                    SM.inc(v.tensor_add(
                        dtl_sb[0:NST, S * b + SW * tc:S * b + SW * (tc + 1)],
                        dtl_sb[0:NST, S * b + SW * tc:S * b + SW * (tc + 1)],
                        PS_B[0][0:NST, 0:SW]), "pa_dlev")
                    v.wait_ge(SM.h["pa_db"], c + 1)
                    SM.inc(v.tensor_add(
                        bmx_sb[0:NST, S * b + SW * tc:S * b + SW * (tc + 1)],
                        bmx_sb[0:NST, S * b + SW * tc:S * b + SW * (tc + 1)],
                        PS_B[1][0:NST, 0:SW]), "pa_dlev")
            SM.wait(v, "vch")
            v.wait_ge(SM.h["pa_dlev"], SM.c["pa_dlev"])
            SM.inc(v.tensor_copy(dtl_bf[:], dtl_sb[:]), "vch")
            SM.wait(v, "vch")
            SM.inc(v.tensor_copy(bm_bf[:], bmx_sb[:]), "pa_dbf")
            for didx in range(DJ * TCH):
                dj = didx // TCH
                tc = didx % TCH
                g = didx // SCH
                v.wait_ge(SM.h["pa_dmm"], didx + 1)
                if didx >= 1:
                    v.wait_ge(SM.h["pa_dte"], didx)
                SM.inc(v.tensor_scalar(sp_r[:, 0:SW], PS_A[didx % 3][:, 0:SW],
                                       bdt_sb[:, dj:dj + 1], 0.0,
                                       ALU.add, ALU.max), "pa_spr")
                v.wait_ge(SM.h["pa_spr"], didx + 1)
                v.wait_ge(SM.h["pa_spl"], didx + 1)
                if g >= 1 and didx % SCH == 0:
                    v.wait_ge(SM.h["pa_sp"], 16 * (2 * DJ * B_LOC + g))
                SM.inc(v.tensor_add(dt_half[:, SW * (tc % SCH):SW * (tc % SCH + 1)],
                                    sp_l[:, 0:SW], sp_r[:, 0:SW]), "pa_dte")

        @blk.scalar
        def _(sc):
            SM.wait(sc, "k_ld", K_LD_T)
            sc.wait_ge(SM.h["p0_sp"], 16 * NT)
            for q in range(DJ * B_LOC):
                dj, b = q // B_LOC, q % B_LOC
                sc.wait_ge(SM.h["pa_cv"], q + 1)
                if q >= 1:
                    sc.wait_ge(SM.h["pa_xbf"], q)
                    sc.wait_ge(SM.h["pa_sp"], 16 * q)
                SM.inc(sc.activation(out=xc_sil[:], in_=xc_acc[:], func=AF.Silu),
                       "pa_sil")
                sc.wait_ge(SM.h["pa_sil"], q + 1)
                SM.inc(sc.dma_start(out=xc_sp[dj * 128:(dj + 1) * 128,
                                              b * S:(b + 1) * S],
                                    in_=xc_sil[:]), "pa_sp", 16)
            for fb in range(DJ, 16):
                djz = fb - DJ
                for b in range(B_LOC):
                    zq = (fb - DJ) * B_LOC + b
                    if zq >= 1:
                        sc.wait_ge(SM.h["pa_sp"],
                                   16 * (DJ * B_LOC + zq))
                    for tc in range(SCH):
                        idx = (fb * B_LOC + b) * SCH + tc
                        sc.wait_ge(SM.h["pa_mm"], idx + 1)
                        SM.inc(sc.activation(out=sz_half[:, SW * tc:SW * (tc + 1)],
                                             in_=PS_A[idx % 3][:, 0:SW],
                                             func=AF.Silu), "pa_evz")
                    sc.wait_ge(SM.h["pa_evz"], SM.c["pa_evz"])
                    SM.inc(sc.dma_start(out=sz_sp[djz * 128:(djz + 1) * 128,
                                                  b * S:(b + 1) * S],
                                        in_=sz_half[:]), "pa_sp", 16)
            for didx in range(DJ * TCH):
                dj = didx // TCH
                tc = didx % TCH
                sc.wait_ge(SM.h["pa_dmm"], didx + 1)
                sc.wait_ge(SM.h["pa_spr"], didx + 1)
                SM.inc(sc.activation(out=sp_u[:, 0:SW], in_=PS_A[didx % 3][:, 0:SW],
                                     func=AF.Abs, bias=bdt_sb[:, dj:dj + 1]),
                       "pa_spu")
                sc.wait_ge(SM.h["pa_spu"], didx + 1)
                if didx >= 1:
                    sc.wait_ge(SM.h["pa_dte"], didx)
                SM.inc(sc.activation(out=sp_l[:, 0:SW], in_=sp_u[:, 0:SW],
                                     func=AF.Exp, scale=-1.0), "sch")
                SM.wait(sc, "sch")
                SM.inc(sc.activation(out=sp_l[:, 0:SW], in_=sp_l[:, 0:SW],
                                     func=AF.Ln, bias=1.0), "pa_spl")
                if tc % SCH == SCH - 1:
                    b = (didx % TCH) // SCH
                    g = didx // SCH
                    sc.wait_ge(SM.h["pa_dte"], didx + 1)
                    SM.inc(sc.dma_start(out=dt_sp[dj * 128:(dj + 1) * 128,
                                                  b * S:(b + 1) * S],
                                        in_=dt_half[:]), "pa_sp", 16)

        PA_MM_T = SM.c["pa_mm"]
        PA_SP_T = SM.c["pa_sp"]

        PA_DT_T = SM.c["pa_dte"]
        phA.close()
        ph0a.close()
        cvt_scope.close()
        phFS = ExitStack()
        woutT_sb = phFS.enter_context(nc.sbuf_tensor("woutT_sb", [128, DJ * 1024], BF16))
        SCH2 = S // 512

        @blk.sync
        def _(sy):
            wait_conv(sy, n_conv)
            sy.wait_ge(SM.h["pa_sp"], PA_SP_T)
            sy.wait_ge(SM.h["pa_dte"], PA_DT_T)
            sy.wait_ge(SM.h["pa_spu"], PA_DT_T)
            for dj in range(DJ):
                SM.inc(sy.dma_start(out=woutT_sb[:, 1024 * dj:1024 * (dj + 1)],
                                    in_=woutTs[dj * 128:(dj + 1) * 128, :]),
                       "wo_ld", 16)


        phF = ExitStack()
        pf = lambda name, shape, dt: phF.enter_context(nc.sbuf_tensor(name, shape, dt))
        slotA = pf("slotA", [128, JB * D], BF16)
        slotB = pf("slotB", [128, DJ * TBLK], BF16)
        xhat_dt = pf("xhat_dt", [128, DJ * TBLK], BF16)
        a_big = pf("a_big", [128, FJ * TBLK], BF16)
        w1_buf = [pf(f"w1b{i}", [128, 4096], BF16) for i in range(2)]
        w2_buf = [pf(f"w2b{i}", [128, 2048], BF16) for i in range(2)]
        grep_sb = pf("grep_sb", [128, TBLK], F32)
        out_acc = pf("out_acc", [128, DJ * TBLK], BF16)
        fma_tmp = pf("fma_tmp", [128, 512], F32)
        xrow = [pf(f"xrow{i}", [128, D], F32) for i in range(2)]
        orow = [pf(f"orow{i}", [128, D], F32) for i in range(2)]
        dt_ch = [pf(f"dt_ch{i}", [128, 512], F32) for i in range(2)]
        xc_ch = [pf(f"xc_ch{i}", [128, 512], F32) for i in range(2)]
        szb_ch = [pf(f"szb_ch{i}", [128, 512], BF16) for i in range(2)]
        a_ch = [pf(f"a_ch{i}", [128, 512], F32) for i in range(2)]
        bx_ch = pf("bx_ch", [128, 512], F32)
        h_ch = [pf(f"h_ch{i}", [128, 512], F32) for i in range(2)]
        yac_ch = pf("yac_ch", [128, 512], F32)
        gy_one = pf("gy2_0", [128, DJ * 512], BF16)
        gy_sb2 = [gy_one, gy_one]
        brep_sb = pf("brep_sb", [128, NST * 512], BF16)
        carry = pf("carry", [128, DJ * NST], F32)
        sc_tmp = pf("sc_tmp", [128, 1], F32)
        NU = NTB * DJ * NST

        GEL_PER_E = FC * 4 * T2
        FMA_PER_E = DJ * T2

        @blk.sync
        def _(sy):
            sy.wait_ge(SM.h["p0_sp"], P0_SP_T)
            sy.wait_ge(SM.h["pa_sp"], PA_SP_T)
            sy.wait_ge(SM.h["pa_dte"], PA_DT_T)
            sy.wait_ge(SM.h["pa_spu"], PA_DT_T)
            wait_conv(sy, n_conv)
            w1i = 0
            for tb in range(NTB):
                if tb >= 1:
                    sy.wait_ge(SM.h["f_oev"], JB * tb)
                SM.inc(sy.dma_start(
                    out=slotA[:],
                    in_=xn_sp[tb * TBLK:(tb + 1) * TBLK, :]
                    .rearrange("(j p) d -> p j d", p=128)), "f_xn", 16)
                for e in range(E):
                    for fc in range(FC):
                        sl = w1i % 2
                        if w1i >= 2:
                            sy.wait_ge(SM.h["f_mm1"], 4 * T2 * (w1i - 1))
                        r0 = (e * FC + fc) * 128
                        SM.inc(sy.dma_start(out=w1_buf[sl][:],
                                            in_=w1s[r0:r0 + 128, :]), f"f_w1{sl}", 16)
                        w1i += 1
                for j in range(JB):
                    jg = tb * JB + j
                    sl = jg % 2
                    if jg >= 2:
                        sy.wait_ge(SM.h["f_oev"], jg - 1)
                    SM.inc(sy.dma_start(
                        out=xrow[sl][:],
                        in_=x_h[tb * TBLK + j * 128:tb * TBLK + (j + 1) * 128, :]),
                        f"f_xr{sl}", 16)

        @blk.gpsimd
        def _(g):
            g.wait_ge(SM.h["pa_sp"], PA_SP_T)
            g.wait_ge(SM.h["pa_dte"], PA_DT_T)
            g.wait_ge(SM.h["pa_spu"], PA_DT_T)
            w2i = 0
            for tb in range(NTB):
                for e in range(E):
                    for dj in range(DJ):
                        for h2 in range(2):
                            sl = w2i % 2
                            if w2i >= 2:
                                g.wait_ge(SM.h["f_mm2"],
                                          ((w2i - 2) // 2) * T2 + T2)
                            r0 = (e * DJ + dj) * 128
                            SM.inc(g.dma_start(
                                out=w2_buf[sl][:],
                                in_=w2s[r0:r0 + 128, 2048 * h2:2048 * (h2 + 1)]),
                                f"f_w2{sl}", 16)
                            w2i += 1

        @blk.vector
        def _(v):
            scan_state = {"u": 0}
            def scan_unit():
                u = scan_state["u"]
                if u >= NU:
                    return
                scan_state["u"] = u + 1
                tb_, r_ = divmod(u, DJ * NST)
                dj_, n_ = divmod(r_, NST)
                c_ = tb_ % SCH2
                sl_ = dj_ % 2
                v.wait_ge(SM.h["s_ach"], u + 1)
                v.wait_ge(SM.h["s_brev"], tb_ * NST + n_ + 1)
                v.wait_ge(SM.h["s_ld"], 48 * (tb_ * DJ + dj_ + 1))
                SM.inc(v.tensor_mul(bx_ch[:], xc_ch[sl_][:],
                                    brep_sb[:, 512 * n_:512 * (n_ + 1)]), "vch")
                SM.wait(v, "vch")
                col = dj_ * NST + n_
                if c_ > 0:
                    SM.inc(v.tensor_mul(sc_tmp[:], a_ch[n_ % 2][:, 0:1],
                                        carry[:, col:col + 1]), "vch")
                    SM.wait(v, "vch")
                    SM.inc(v.tensor_add(bx_ch[:, 0:1], bx_ch[:, 0:1], sc_tmp[:]), "vch")
                    SM.wait(v, "vch")
                if u >= 2:
                    v.wait_ge(SM.h["s_cc"], u - 1)
                SM.inc(v.tensor_tensor_scan(h_ch[n_ % 2][:], a_ch[n_ % 2][:], bx_ch[:],
                                            0.0, ALU.mult, ALU.add), "s_scan")
                SM.wait(v, "s_scan")
                if n_ == 0:
                    SM.inc(v.tensor_copy(yac_ch[:], h_ch[0][:]), "vch")
                else:
                    SM.inc(v.tensor_add(yac_ch[:], yac_ch[:], h_ch[n_ % 2][:]), "vch")
                SM.wait(v, "vch")
                if n_ == NST - 1:
                    if tb_ >= 1:
                        v.wait_ge(SM.h["f_wo"], DJ * tb_)
                    SM.inc(v.tensor_mul(gy_sb2[tb_ % 2][:, 512 * dj_:512 * (dj_ + 1)],
                                        yac_ch[:], szb_ch[sl_][:]), "s_gy")
            for tb in range(NTB):
                v.wait_ge(SM.h["f_xn"], 16 * (tb + 1))
                if tb >= 1:
                    v.wait_ge(SM.h["f_tp"], 8 * tb)
                for j in range(JB):
                    jt = tb * JB + j
                    SM.inc(v.tensor_scalar(slotB[:, D * j:D * (j + 1)],
                                           slotA[:, D * j:D * (j + 1)],
                                           mu2_all[:, jt:jt + 1],
                                           rs2_all[:, jt:jt + 1],
                                           ALU.subtract, ALU.mult), "f_xh")
                for i in range(DJ):
                    v.wait_ge(SM.h["f_tp"], 8 * tb + i + 1)
                    SM.inc(v.tensor_copy(xhat_dt[:, TBLK * i:TBLK * (i + 1)],
                                         PS_T[:, 0:JB * 128]), "f_xhev")
                for u in range(DJ * T2):
                    v.wait_ge(SM.h["f_init"], tb * DJ * T2 + u + 1)
                    dj, tc = u // T2, u % T2
                    SM.inc(v.tensor_copy(
                        out_acc[:, TBLK * dj + 512 * tc:TBLK * dj + 512 * (tc + 1)],
                        PS_A[u % 3][:, 0:512]), "f_initev")
                for e in range(E):
                    ee = tb * E + e
                    v.wait_ge(SM.h["f_gr"], ee + 1)
                    if TBLK > 512:
                        SM.inc(v.tensor_copy(grep_sb[:, 0:512], PS_B[0][:, 0:512]), "vch")
                        SM.wait(v, "vch")
                        SM.inc(v.tensor_copy(grep_sb[:, 512:TBLK], PS_B[0][:, 512:TBLK]), "f_grev")
                    else:
                        SM.inc(v.tensor_copy(grep_sb[:], PS_B[0][:, 0:TBLK]), "f_grev")
                    if e == 0:
                        v.wait_ge(SM.h["f_tp"], 8 * (tb + 1))
                        SM.wait(v, "f_xhev")
                    else:
                        v.wait_ge(SM.h["f_mm1"], 4 * T2 * FC * ee)
                    for dj in range(DJ):
                        SM.inc(v.tensor_scalar(
                            slotB[:, TBLK * dj:TBLK * (dj + 1)],
                            xhat_dt[:, TBLK * dj:TBLK * (dj + 1)],
                            lng_sb[:, E * dj + e:E * dj + e + 1],
                            lnb_sb[:, E * dj + e:E * dj + e + 1],
                            ALU.mult, ALU.add),
                            "f_h" if dj == DJ - 1 else "vch")
                    for dj in range(DJ):
                        djg = ee * DJ + dj
                        psY = PS_B[dj % 2]
                        for tc in range(T2):
                            v.wait_ge(SM.h["f_mm2"], T2 * djg + tc + 1)
                            SM.wait(v, "f_fma")
                            SM.inc(v.tensor_mul(fma_tmp[:],
                                                psY[:, 512 * tc:512 * (tc + 1)],
                                                grep_sb[:, 512 * tc:512 * (tc + 1)]),
                                   "vch")
                            SM.wait(v, "vch")
                            SM.inc(v.tensor_add(
                                out_acc[:, TBLK * dj + 512 * tc:
                                        TBLK * dj + 512 * (tc + 1)],
                                out_acc[:, TBLK * dj + 512 * tc:
                                        TBLK * dj + 512 * (tc + 1)],
                                fma_tmp[:]), "f_fma")
                        scan_unit()
                        scan_unit()
                while scan_state["u"] < (tb + 1) * DJ * NST:
                    scan_unit()
                for doj in range(DJ):
                    v.wait_ge(SM.h["f_wo"], DJ * tb + doj + 1)
                    SM.inc(v.tensor_copy(slotB[:, TBLK * doj:TBLK * (doj + 1)],
                                         PS_A[doj % 3][:, 0:512]), "f_woev")
                SM.wait(v, "f_fma")
                SM.wait(v, "f_woev")
                for dj in range(DJ):
                    SM.inc(v.tensor_add(out_acc[:, TBLK * dj:TBLK * (dj + 1)],
                                        out_acc[:, TBLK * dj:TBLK * (dj + 1)],
                                        slotB[:, TBLK * dj:TBLK * (dj + 1)]),
                           "f_aadd" if dj == DJ - 1 else "vch")
                for j in range(JB):
                    jg = tb * JB + j
                    sl = jg % 2
                    v.wait_ge(SM.h["f_otp"], JB * tb + j + 1)
                    v.wait_ge(SM.h[f"f_xr{sl}"], 16 * (jg // 2 + 1))
                    if jg >= 2:
                        v.wait_ge(SM.h[f"f_out{jg % 2}"], 16 * (jg // 2))
                    SM.inc(v.tensor_add(orow[sl][:], PS_T[:, 0:D],
                                        xrow[sl][:]), "vch")
                    SM.wait(v, "vch")
                    SM.inc(v.tensor_add(orow[sl][:], orow[sl][:],
                                        slotA[:, D * j:D * (j + 1)]), "f_oev")

        @blk.tensor
        def _(t):
            t.wait_ge(SM.h["p0_gev"], NT)
            t.wait_ge(SM.h["pa_spu"], PA_DT_T)
            t.wait_ge(SM.h["pa_spr"], PA_DT_T)
            t.wait_ge(SM.h["pa_dte"], PA_DT_T)
            t.wait_ge(SM.h["pa_dbf"], 1)
            SM.wait(t, "k2_ld", 48)
            SM.wait(t, "wo_ld")
            for tb in range(NTB):
                b_i, c_i = tb // SCH2, tb % SCH2
                for n in range(NST):
                    if tb * NST + n >= 2:
                        t.wait_ge(SM.h["s_brev"], tb * NST + n - 1)
                    SM.inc(t.matmul(PS_B[n % 2][:, 512:1024],
                                    oh16[:, 128 * n:128 * (n + 1)],
                                    bm_bf[0:NST, b_i * S + 512 * c_i:b_i * S + 512 * (c_i + 1)],
                                    start=True, stop=True), "s_br")
                t.wait_ge(SM.h["f_xh"], JB * (tb + 1))
                for i in range(DJ):
                    if i == 0 and tb >= 1:
                        t.wait_ge(SM.h["f_oev"], JB * tb)
                    if 8 * tb + i >= 1:
                        t.wait_ge(SM.h["f_xhev"], 8 * tb + i)
                    for j in range(JB):
                        ins = t.transpose(PS_T[:, 128 * j:128 * (j + 1)],
                                          slotB[:, D * j + 128 * i:D * j + 128 * (i + 1)],
                                          identbf[:])
                        if j == JB - 1:
                            SM.inc(ins, "f_tp")
                if tb >= 1:
                    t.wait_ge(SM.h["f_gel"], GEL_PER_E * E * tb)
                    t.wait_ge(SM.h["f_woev"], DJ * tb)
                for u in range(DJ * T2):
                    dj, tc = u // T2, u % T2
                    if tb * DJ * T2 + u >= 3:
                        t.wait_ge(SM.h["f_initev"], tb * DJ * T2 + u - 2)
                    SM.inc(t.matmul(PS_A[u % 3][:, 0:512],
                                    b2_sb[0:E, 128 * dj:128 * (dj + 1)],
                                    gatesTbf[0:E, TBLK * tb + 512 * tc:
                                             TBLK * tb + 512 * (tc + 1)],
                                    start=True, stop=True), "f_init")
                for e in range(E):
                    ee = tb * E + e
                    t.wait_ge(SM.h["f_fma"], T2 * DJ * ee)
                    if e == 0 and tb >= 1:
                        t.wait_ge(SM.h["f_oev"], JB * tb)
                    if ee >= 1:
                        t.wait_ge(SM.h["f_grev"], ee)
                    for tc in range(T2):
                        ins = t.matmul(PS_B[0][:, 512 * tc:512 * (tc + 1)],
                                       oh8[:, 128 * e:128 * (e + 1)],
                                       gatesTbf[0:E, TBLK * tb + 512 * tc:
                                                TBLK * tb + 512 * (tc + 1)],
                                       start=True, stop=True)
                    SM.inc(ins, "f_gr")
                    t.wait_ge(SM.h["f_h"], ee + 1)
                    t.wait_ge(SM.h["f_grev"], ee + 1)
                    if e == 0:
                        t.wait_ge(SM.h["f_initev"], (tb + 1) * DJ * T2)
                    for fc in range(FC):
                        w1i = ee * FC + fc
                        sl = w1i % 2
                        t.wait_ge(SM.h[f"f_w1{sl}"], 16 * (w1i // 2 + 1))
                        for fi in range(4):
                            gidx = (ee * FC + fc) * 4 + fi
                            pool = [PS_A[0], PS_A[1], PS_A[2], PS_B[0]]
                            pr = [pool[2 * (fi % 2)], pool[2 * (fi % 2) + 1]]
                            ofs = [0, 0] if fi % 2 == 0 else [0, 512]
                            if gidx >= 2:
                                t.wait_ge(SM.h["f_gel"], T2 * gidx - T2)
                            for dj in range(DJ):
                                for tc in range(T2):
                                    ins = t.matmul(
                                        pr[tc][:, ofs[tc]:ofs[tc] + 512],
                                        w1_buf[sl][:, 512 * dj + 128 * fi:
                                                   512 * dj + 128 * (fi + 1)],
                                        slotB[:, TBLK * dj + 512 * tc:
                                              TBLK * dj + 512 * (tc + 1)],
                                        start=(dj == 0), stop=(dj == DJ - 1))
                                    if dj == DJ - 1:
                                        SM.inc(ins, "f_mm1")
                    t.wait_ge(SM.h["f_gel"], GEL_PER_E * (ee + 1))
                    for dj in range(DJ):
                        djg = ee * DJ + dj
                        psY = PS_B[dj % 2]
                        if djg >= 2:
                            t.wait_ge(SM.h["f_fma"], T2 * djg - T2)
                        for h2 in range(2):
                            w2i = djg * 2 + h2
                            sl = w2i % 2
                            t.wait_ge(SM.h[f"f_w2{sl}"], 16 * (w2i // 2 + 1))
                            for fj in range(16):
                                fjg = 16 * h2 + fj
                                for tc in range(T2):
                                    ins = t.matmul(
                                        psY[:, 512 * tc:512 * (tc + 1)],
                                        w2_buf[sl][:, 128 * fj:128 * (fj + 1)],
                                        a_big[:, TBLK * fjg + 512 * tc:
                                              TBLK * fjg + 512 * (tc + 1)],
                                        start=(fjg == 0), stop=(fjg == FJ - 1))
                                    if fjg == FJ - 1:
                                        SM.inc(ins, "f_mm2")
                t.wait_ge(SM.h["f_gel"], GEL_PER_E * E * (tb + 1))
                t.wait_ge(SM.h["s_gy"], DJ * (tb + 1))
                for doj in range(DJ):
                    if DJ * tb + doj >= 3:
                        t.wait_ge(SM.h["f_woev"], DJ * tb + doj - 2)
                    for dij in range(DJ):
                        ins = t.matmul(PS_A[doj % 3][:, 0:512],
                                       woutT_sb[:, 1024 * dij + 128 * doj:
                                                1024 * dij + 128 * (doj + 1)],
                                       gy_sb2[tb % 2][:, 512 * dij:512 * (dij + 1)],
                                       start=(dij == 0), stop=(dij == DJ - 1))
                    SM.inc(ins, "f_wo")
                t.wait_ge(SM.h["f_aadd"], tb + 1)
                for j in range(JB):
                    if JB * tb + j >= 1:
                        t.wait_ge(SM.h["f_oev"], JB * tb + j)
                    for i in range(DJ):
                        ins = t.transpose(PS_T[:, 128 * i:128 * (i + 1)],
                                          out_acc[:, TBLK * i + 128 * j:
                                                  TBLK * i + 128 * (j + 1)],
                                          identbf[:])
                        if i == DJ - 1:
                            SM.inc(ins, "f_otp")

        @blk.scalar
        def _(sc):
            def scan_loads(tb_, dj_):
                b_i, c_i = tb_ // SCH2, tb_ % SCH2
                g = tb_ * DJ + dj_
                sl_ = dj_ % 2
                if g >= 2:
                    sc.wait_ge(SM.h["s_gy"], g - 1)
                SM.inc(sc.dma_start(out=dt_ch[sl_][:],
                                    in_=dt_sp[dj_ * 128:(dj_ + 1) * 128,
                                              b_i * S + 512 * c_i:b_i * S + 512 * (c_i + 1)]),
                       "s_ld", 16)
                SM.inc(sc.dma_start(out=xc_ch[sl_][:],
                                    in_=xc_sp[dj_ * 128:(dj_ + 1) * 128,
                                              b_i * S + 512 * c_i:b_i * S + 512 * (c_i + 1)]),
                       "s_ld", 16)
                SM.inc(sc.dma_start(out=szb_ch[sl_][:],
                                    in_=sz_sp[dj_ * 128:(dj_ + 1) * 128,
                                              b_i * S + 512 * c_i:b_i * S + 512 * (c_i + 1)]),
                       "s_ld", 16)
            sc.wait_ge(SM.h["pa_sp"], PA_SP_T)
            scan_loads(0, 0)
            for tb in range(NTB):
                for e in range(E):
                    ee = tb * E + e
                    if e == 0:
                        for n in range(NST):
                            sc.wait_ge(SM.h["s_br"], tb * NST + n + 1)
                            if tb >= 1 and n == 0:
                                sc.wait_ge(SM.h["s_gy"], DJ * tb)
                            SM.inc(sc.copy(brep_sb[:, 512 * n:512 * (n + 1)],
                                           PS_B[n % 2][:, 512:1024]), "s_brev")
                        if tb >= 1:
                            scan_loads(tb, 1)
                    for fc in range(FC):
                        for fi in range(4):
                            gidx = (ee * FC + fc) * 4 + fi
                            pool = [PS_A[0], PS_A[1], PS_A[2], PS_B[0]]
                            pr = [pool[2 * (fi % 2)], pool[2 * (fi % 2) + 1]]
                            ofs = [0, 0] if fi % 2 == 0 else [0, 512]
                            fb = fc * 4 + fi
                            for tc in range(T2):
                                sc.wait_ge(SM.h["f_mm1"], T2 * gidx + tc + 1)
                                SM.inc(sc.activation(
                                    out=a_big[:, TBLK * fb + 512 * tc:
                                              TBLK * fb + 512 * (tc + 1)],
                                    in_=pr[tc][:, ofs[tc]:ofs[tc] + 512],
                                    func=AF.Gelu,
                                    bias=b1_sb[:, E * fb + e:E * fb + e + 1]),
                                    "f_gel")
                    dj_ = e
                    nxt = tb * DJ + dj_ + 1
                    if nxt < NTB * DJ and not (nxt % DJ == 1 and nxt >= DJ):
                        scan_loads(nxt // DJ, nxt % DJ)
                    for n in range(NST):
                        u = tb * DJ * NST + dj_ * NST + n
                        sc.wait_ge(SM.h["s_ld"], 48 * (tb * DJ + dj_ + 1))
                        if u >= 2:
                            sc.wait_ge(SM.h["s_scan"], u - 1)
                        SM.inc(sc.activation(out=a_ch[n % 2][:], in_=dt_ch[dj_ % 2][:],
                                             func=AF.Exp,
                                             scale=a_sb[:, NST * dj_ + n:NST * dj_ + n + 1]),
                               "s_ach")
                        if n >= 1:
                            sc.wait_ge(SM.h["s_scan"], u)
                            SM.inc(sc.copy(carry[:, dj_ * NST + (n - 1):dj_ * NST + n],
                                           h_ch[(n - 1) % 2][:, 511:512]), "s_cc")
                    ul = tb * DJ * NST + dj_ * NST + NST - 1
                    sc.wait_ge(SM.h["s_scan"], ul + 1)
                    SM.inc(sc.copy(carry[:, dj_ * NST + NST - 1:dj_ * NST + NST],
                                   h_ch[(NST - 1) % 2][:, 511:512]), "s_cc")
                for j in range(JB):
                    jg = tb * JB + j
                    sl = jg % 2
                    sc.wait_ge(SM.h["f_oev"], jg + 1)
                    SM.inc(sc.dma_start(
                        out=out_h[tb * TBLK + j * 128:tb * TBLK + (j + 1) * 128, :],
                        in_=orow[sl][:]), f"f_out{jg % 2}", 16)
            sc.wait_ge(SM.h["f_out0"], 16 * ((NT + 1) // 2))
            sc.wait_ge(SM.h["f_out1"], 16 * (NT // 2))

        phF.close()
        phFS.close()

    return nc


_bf = ml_dtypes.bfloat16


def _prep_consts():
    ident = np.eye(128, dtype=np.float32)
    identbf = np.eye(128, dtype=_bf)
    oh16 = np.zeros((NST, NST * 128), dtype=_bf)
    for n in range(NST):
        oh16[n, 128 * n:128 * (n + 1)] = 1
    oh8 = np.zeros((E, E * 128), dtype=_bf)
    for e in range(E):
        oh8[e, 128 * e:128 * (e + 1)] = 1
    return ident, identbf, oh16, oh8


def _blockify(a):
    dd, w = a.shape
    nj = dd // 128
    return np.ascontiguousarray(
        a.reshape(nj, 128, w).transpose(1, 0, 2).reshape(128, nj * w))


def kernel(L0_attn_x, L0_attn_W_in, L0_attn_conv_w, L0_attn_W_x, L0_attn_W_dt,
           L0_attn_b_dt, L0_attn_A_log, L0_attn_W_out, L0_ffn_router_W,
           L0_ffn_router_b, L0_ffn_ln_g, L0_ffn_ln_b, L0_ffn_W1, L0_ffn_b1,
           L0_ffn_W2, L0_ffn_b2):
    f32 = np.float32
    x = np.ascontiguousarray(np.asarray(L0_attn_x, f32))
    Bg = x.shape[0]
    w_inT = np.ascontiguousarray(np.asarray(L0_attn_W_in, f32).T)
    w_outT = np.ascontiguousarray(np.asarray(L0_attn_W_out, f32).T)
    w_xT = np.asarray(L0_attn_W_x, f32).T
    w_xT_r = _blockify(w_xT)
    w_dtT = np.ascontiguousarray(np.asarray(L0_attn_W_dt, f32).T)
    b_dt_r = _blockify(np.asarray(L0_attn_b_dt, f32).reshape(D, 1))
    a_log_r = _blockify(np.asarray(L0_attn_A_log, f32))
    conv_r = _blockify(np.asarray(L0_attn_conv_w, f32).reshape(D, KCV))
    rw_r = _blockify(np.asarray(L0_ffn_router_W, f32))
    rb_rep = np.ascontiguousarray(
        np.broadcast_to(np.asarray(L0_ffn_router_b, f32).reshape(1, E), (128, E)))
    lng_r = _blockify(np.ascontiguousarray(np.asarray(L0_ffn_ln_g, f32).T))
    lnb_r = _blockify(np.ascontiguousarray(np.asarray(L0_ffn_ln_b, f32).T))
    b1_r = _blockify(np.ascontiguousarray(np.asarray(L0_ffn_b1, f32).T))
    b2 = np.ascontiguousarray(np.asarray(L0_ffn_b2, f32))
    w1 = np.ascontiguousarray(np.asarray(L0_ffn_W1, f32).reshape(E * D, FF))
    w2 = np.ascontiguousarray(np.asarray(L0_ffn_W2, f32).reshape(E * FF, D))
    ident, identbf, oh16, oh8 = _prep_consts()
    fvals = np.ascontiguousarray(np.broadcast_to(np.array([EPS, -0.5, -1.0, 1.0, 0, 0, 0, 0], np.float32), (128, 8)))

    nc = build_program()
    shared = dict(w_inT=w_inT, w_outT=w_outT, w_xT_r=w_xT_r, w_dtT=w_dtT,
                  b_dt_r=b_dt_r, a_log_r=a_log_r, conv_r=conv_r, rw_r=rw_r,
                  rb_rep=rb_rep, lng_r=lng_r, lnb_r=lnb_r, b1_r=b1_r, b2=b2,
                  w1=w1, w2=w2, ident=ident, identbf=identbf, oh16=oh16, oh8=oh8,
                  fvals=fvals)
    in_maps = []
    per_core = Bg // NCORES
    for c in range(NCORES):
        xc = np.ascontiguousarray(
            x[c * per_core:(c + 1) * per_core].reshape(per_core * x.shape[1], D))
        in_maps.append(dict(shared, x=xc))
    res = run_bass_kernel_spmd(nc, in_maps, core_ids=list(range(NCORES)))
    outs = [np.asarray(r["out"], np.float32).reshape(per_core, x.shape[1], D)
            for r in res.results]
    return np.concatenate(outs, axis=0)
